# version 06 (final)
# baseline (speedup 1.0000x reference)
import numpy as np
import concourse.bass as bass
import concourse.mybir as mybir
from concourse.bass_utils import run_bass_kernel_spmd
from concourse.alu_op_type import AluOpType as OP

F32 = mybir.dt.float32
BF16 = mybir.dt.bfloat16
AF = mybir.ActivationFunctionType
AX = mybir.AxisListType

MAGIC = 12582912.0
NEG = -1.0e38
NCORES = 8
BL = 2
T = 2048
D = 1024
NCH = D // 128
H, LHD, DHD, NB, BW = 16, 4, 64, 64, 32
SHD = H - LHD
FI = 3072
NFC = FI // 128
TSL = 512
NSL = T // TSL
QMAX = 32767.0
EPS = 1e-5

DBG = False
PHASES = None
SERIAL = False
LOCAL_BLOCKS = False
LOCAL_ITERS = None
RG = NCORES


class Sem:
    def __init__(self, nc, name):
        self.h = nc.alloc_semaphore(name=name)
        self.v = 0

    def dma(self, ins):
        self.v += 16
        ins.then_inc(self.h, 16)
        return self.v

    def inc(self, ins, n=1):
        self.v += n
        ins.then_inc(self.h, n)
        return self.v

    def wait(self, eng, v=None):
        eng.wait_ge(self.h, self.v if v is None else v)


def _vs(c, v, ins):
    s = c.sems["s_nw"]
    s.inc(ins)
    s.wait(v)
    return ins


class Arena:
    def __init__(self, nc, name, n_elems, dtype):
        self.t = nc.alloc_sbuf_tensor(name, [128, n_elems], dtype)
        self.n = n_elems

    def viewp(self, off, p0, *shape):
        n = 1
        for s in shape[1:]:
            n *= s
        assert off + n <= self.n, f"arena overflow {off}+{n}>{self.n}"
        ap = self.t[p0:p0 + shape[0], off:off + n]
        if len(shape) > 2:
            pat = " ".join(f"d{i}" for i in range(1, len(shape)))
            ap = ap.rearrange(f"p ({pat}) -> p {pat}",
                              **{f"d{i}": s for i, s in enumerate(shape[2:], 2)})
        return ap

    def view(self, off, *shape):
        assert shape[0] <= 128
        n = 1
        for s in shape[1:]:
            n *= s
        assert off + n <= self.n, f"arena overflow {off}+{n}>{self.n}"
        ap = self.t[0:shape[0], off:off + n]
        if len(shape) > 2:
            pat = " ".join(f"d{i}" for i in range(1, len(shape)))
            ap = ap.rearrange(f"p ({pat}) -> p {pat}",
                              **{f"d{i}": s for i, s in enumerate(shape[2:], 2)})
        return ap


class Ctx:
    pass


def build_nc(dbg=None, phases=None):
    if dbg is None:
        dbg = DBG
    if phases is None:
        phases = PHASES
    c = Ctx()
    nc = bass.Bass("TRN2", target_bir_lowering=False, debug=False,
                   detect_race_conditions=False)
    c.nc = nc
    c.dbg = dbg

    def on(name):
        return phases is None or name in phases

    io = {}
    c.io = io

    def din(name, shape, dt=F32):
        io[name] = nc.dram_tensor(name, shape, dt, kind="ExternalInput")
        return io[name]

    def dout(name, shape, dt=F32):
        io[name] = nc.dram_tensor(name, shape, dt, kind="ExternalOutput")
        return io[name]

    def dint(name, shape, dt=F32):
        io[name] = nc.dram_tensor(name, shape, dt,
                                  kind="ExternalOutput" if dbg else "Internal")
        return io[name]

    din("xT", [BL, D, T])
    din("w_q", [D, D])
    din("w_kv", [D, 2 * D])
    din("w_out", [D, D])
    din("b_out", [128, NCH])
    din("sort_w", [2, 64, SHD, 2, 64])
    din("cT", [D, BL])
    din("wmod", [D, 6 * D])
    din("wg_nat", [FI, D])
    din("wu_nat", [FI, D])
    din("wd_nat", [D, FI])
    dout("outT", [BL, D, T])

    dint("qkv_sp", [BL, T, 3 * D], BF16)
    dint("x2_sp", [BL, NCH, 128, T])
    dint("h2_sp", [BL, NCH, 128, T], BF16)
    dint("bvr_sp", [BL, 3, 64, 32 * 4 * 64], BF16)
    dint("up_sp", [NFC, 128, BL * T], BF16)
    dint("gate_sp", [NFC, 128, BL * T], BF16)
    dint("g1s_sp", [NFC, 128, BL * T], BF16)
    dint("g2u_sp", [NFC, 128, BL * T], BF16)
    dint("wu_dq", [FI, D], BF16)
    dint("wg_dq", [FI, D], BF16)
    dint("wd_dq", [D, FI], BF16)
    dint("bc_sp", [8, T])
    dint("modr_sp", [BL, 6 * D])
    if dbg:
        dint("ao_sp", [BL, 128, NCH * T], BF16)
        dint("tl_sp", [BL, 4, 2, 128, 2 * T], BF16)
        dint("msk_sp", [128, 3 * 192 + 256 + 136])
    c.ar = {}
    for i in (1, 2, 3, 4):
        c.ar[i] = (nc.dram_tensor(f"ar_in{i}", [1, 8], F32, kind="Internal"),
                   nc.dram_tensor(f"ar_out{i}", [1, 8], F32, kind="Internal"))

    P = {}
    c.P = P
    P["ident_bf"] = nc.alloc_sbuf_tensor("ident_bf", [128, 128], BF16)
    P["ident_f"] = nc.alloc_sbuf_tensor("ident_f", [128, 128], F32)
    P["ones128"] = nc.alloc_sbuf_tensor("ones128", [128, 128], F32)
    P["onecol"] = nc.alloc_sbuf_tensor("onecol", [128, 1], BF16)
    P["mask_loc"] = nc.alloc_sbuf_tensor("mask_loc", [128, 3 * 192], F32)
    P["mask_snk"] = nc.alloc_sbuf_tensor("mask_snk", [128, 256], F32)
    for nm in ("sa", "ca1", "ga", "sb", "cb1", "gb", "gab", "gbs4"):
        P[nm] = nc.alloc_sbuf_tensor(nm, [128, NCH * BL], F32)
    P["bo"] = nc.alloc_sbuf_tensor("bo", [128, NCH], F32)
    P["sw"] = nc.alloc_sbuf_tensor("sw", [128, SHD * 2 * 64], BF16)
    P["sc"] = nc.alloc_sbuf_tensor("scals", [128, 64], F32)
    P["mxc"] = nc.alloc_sbuf_tensor("mxc", [128, 8], F32)
    P["mxt"] = nc.alloc_sbuf_tensor("mxt", [128, 256], F32)
    c.SC = dict(SXI=0, SX=1, QU=2, QG=3, S2U=4, S2G=5, Q3=6, S3=7, Q4=8,
                S4=9, TMP=10, TMP2=11, EPSC=12)

    c.AB = Arena(nc, "AB", 73728, BF16)
    c.AF = Arena(nc, "AF4", 8192, F32)

    c.o_wdq = 0
    c.o_csil = 71680
    c.o_hF = 0
    c.o_vT = 24576
    c.o_wqkv = 40960
    c.o_stg = 65536
    c.o_qF = 0
    c.o_kF = 4096
    c.o_ao = 8192
    c.o_kbm = 40960
    c.o_vbm = 49152
    c.o_bkrF = 57344
    c.o_bvrg = 61440
    c.o_bvrt = 40960
    c.o_a2t = 69632
    c.o_asm = 70656
    c.o_wo = 49152
    c.o_xq = 0
    c.o_wst = 32768
    c.o_fst = 57344
    c.o_ffn2 = 61440
    c.o_wd = 32768
    c.o_g2u = 0
    c.o_oq = 12288

    c.f_xstg = 0
    c.f_pq = 1024
    c.f_x2 = 1024
    c.f_sq = 5120
    c.f_att = 5632
    c.f_sm = 7104
    c.f_ffn = 5120
    c.f_x2t = 6048
    c.f_mod = 0
    c.f_cf = 6144
    c.f_dq = 1024

    sems = {}
    c.sems = sems
    for nm in ("s_ld", "s_w", "s_pe", "s_ev", "s_sp", "s_tl", "s_gp", "s_v",
               "s_a", "s_cc", "s_out", "s_x", "s_y", "s_z", "s_gd", "s_nw"):
        sems[nm] = Sem(nc, nm)

    c.PS = [nc.alloc_psum_tensor(f"ps{i}", [128, 512], F32) for i in range(8)]

    with nc.Block() as blk:
        @blk.gpsimd
        def _(gp):
            gp.dma_reset()
            for s in sems.values():
                gp.sem_clear(s.h)

    _consts(c)
    if on("mod"):
        _mod(c)
    if on("dequant"):
        _dequant(c)
    for s in range(BL):
        if on("ln1"):
            _ln1(c, s)
        if on("qkv"):
            _qkv(c, s)
        if on("attn") or on("attn_local"):
            _local(c, s)
        if on("attn") or on("attn_sink"):
            _sink(c, s)
        if on("proj"):
            _outproj(c, s)
            _h2max(c, s)
    if on("ffn"):
        _ar_scale(c, 1)
        _xqpass(c)
        _f1(c)
        _ar_scale(c, 2)
        _mid1(c)
        _ar_scale(c, 3)
        _mid2(c)
        _ar_scale(c, 4)
        _down(c)

    with nc.Block() as blk:
        @blk.sync
        def _(sy):
            c.sems["s_out"].wait(sy)

    return nc, io, sems


def _consts(c):
    nc, P, AF4 = c.nc, c.P, c.AF
    s_sp, s_gp = c.sems["s_sp"], c.sems["s_gp"]
    with nc.Block() as blk:
        @blk.sync
        def _(sy):
            s_sp.dma(sy.dma_start(out=P["bo"][:, :], in_=c.io["b_out"].ap()))
            s_sp.wait(sy)

        @blk.gpsimd
        def _(gp):
            c.sems["s_gd"].dma(gp.dma_start(
                out=P["sw"][:, :].rearrange("p (h q v) -> p h q v", q=2, v=64),
                in_=c.io["sort_w"].ap().rearrange("a e h q v -> (a e) h q v")))
            c.sems["s_gd"].wait(gp)
            gp.memset(P["ones128"][:], 1.0)
            gp.affine_select(P["ident_f"][:], P["ones128"][:], [[-1, 128]],
                             OP.is_equal, 0.0, base=0, channel_multiplier=1)
            cb = AF4.view(0, 128, 192)
            colmat = AF4.view(768, 128, 128)
            gp.iota(cb, [[1, 6], [0, 32]], channel_multiplier=0,
                    allow_small_or_imprecise_dtypes=True)
            gp.iota(colmat, [[1, 128]], channel_multiplier=0,
                    allow_small_or_imprecise_dtypes=True)
            cs = AF4.view(512, 128, 256)
            s_gp.inc(gp.iota(cs, [[0, 2], [1, 4], [0, 32]],
                             channel_multiplier=0,
                             allow_small_or_imprecise_dtypes=True))

        @blk.tensor
        def _(t):
            s_gp.wait(t)
            colmat = AF4.view(768, 128, 128)
            c.sems["s_pe"].inc(
                t.transpose(c.PS[0][:, 0:128], colmat, P["ident_f"][:]))

        @blk.vector
        def _(v):
            s_gp.wait(v)
            c.sems["s_pe"].wait(v)
            cb = AF4.view(0, 128, 192)
            pcol = AF4.view(192, 128, 1)
            rb = AF4.view(193, 128, 1)
            tmp = AF4.view(194, 128, 1)
            t2 = AF4.view(256, 128, 256)
            cs = AF4.view(512, 128, 256)
            _vs(c, v, v.tensor_copy(pcol, c.PS[0][:, 0:1]))
            v.memset(P["onecol"][:], 1.0)
            v.memset(P["sc"][:, 12:13], EPS)
            v.memset(P["mxc"][:, :], 0.0)
            v.tensor_copy(P["ident_bf"][:], P["ident_f"][:])
            _vs(c, v, v.tensor_scalar(tmp, pcol, 1.0 / 32.0, 0.484375,
                                      OP.mult, OP.subtract))
            _vs(c, v, v.tensor_scalar(rb, tmp, MAGIC, None, OP.add))
            _vs(c, v, v.tensor_scalar(rb, rb, -MAGIC, None, OP.add))
            for mi, lo in enumerate((-1.0, 0.0, 1.0)):
                m = P["mask_loc"][:, mi * 192:(mi + 1) * 192]
                v.tensor_scalar(m, cb, rb[:, 0:1], None, OP.subtract)
                v.tensor_scalar(t2[:, 0:192], m, lo, None, OP.is_ge)
                v.tensor_scalar(m, m, lo + 2.0, None, OP.is_le)
                v.tensor_mul(m, m, t2[:, 0:192])
                v.tensor_scalar(m, m, 1.0, -NEG, OP.subtract, OP.mult)
            m = P["mask_snk"][:, :]
            v.tensor_scalar(m, cs, rb[:, 0:1], None, OP.subtract)
            v.tensor_scalar(t2, m, 0.0, None, OP.is_ge)
            v.tensor_scalar(m, m, 0.0, None, OP.is_le)
            v.tensor_mul(m, m, t2)
            v.tensor_scalar(m, m, 1.0, -NEG, OP.subtract, OP.mult)

    if c.dbg:
        with nc.Block() as blk:
            @blk.sync
            def _(sy):
                s_sp = c.sems["s_sp"]
                s_sp.dma(sy.dma_start(
                    out=c.io["msk_sp"].ap()[:, 0:576],
                    in_=P["mask_loc"][:, :]))
                s_sp.dma(sy.dma_start(
                    out=c.io["msk_sp"].ap()[:, 576:832],
                    in_=P["mask_snk"][:, :]))
                s_sp.dma(sy.dma_start(
                    out=c.io["msk_sp"].ap()[:, 832:840],
                    in_=c.AF.view(192, 128, 8)))
                s_sp.dma(sy.dma_start(
                    out=c.io["msk_sp"].ap()[:, 840:968],
                    in_=c.AF.view(768, 128, 128)))
                s_sp.wait(sy)


def _mod(c):
    nc, P, AB, AF4, PS = c.nc, c.P, c.AB, c.AF, c.PS
    s_ld, s_w, s_pe, s_v = (c.sems[k] for k in ("s_ld", "s_w", "s_pe", "s_v"))
    s_sp = c.sems["s_sp"]
    cF = AF4.view(c.f_cf, 128, NCH, BL)
    csil = AB.view(c.o_csil, 128, NCH, BL)
    wtile = AB.view(c.o_stg, 128, 2, 512)
    modr = AF4.view(c.f_mod, 2, 6 * D)

    with nc.Block() as blk:
        @blk.sync
        def _(sy):
            s_ld.dma(sy.dma_start(
                out=cF, in_=c.io["cT"].ap().rearrange("(ch p) b -> p ch b",
                                                      p=128)))

        @blk.scalar
        def _(se):
            s_ld.wait(se)
            s_a = c.sems["s_a"]
            s_a.inc(se.activation(csil, cF, AF.Silu))

        @blk.gpsimd
        def _(gp):
            c.sems["s_a"].wait(gp)
            pe0 = s_pe.v
            for i in range(12 * NCH):
                n, k = divmod(i, NCH)
                if i >= 2:
                    gp.wait_ge(s_pe.h, pe0 + i - 1)
                s_w.dma(gp.dma_start(
                    out=wtile[:, i % 2, :],
                    in_=c.io["wmod"].ap().rearrange("(ch p) f -> p ch f",
                                                    p=128)
                    [:, k, n * 512:(n + 1) * 512]))

        @blk.tensor
        def _(t):
            w0 = s_w.v - 12 * NCH * 16
            v0 = s_v.v
            for i in range(12 * NCH):
                n, k = divmod(i, NCH)
                t.wait_ge(s_w.h, w0 + (i + 1) * 16)
                if k == 0 and n >= 2:
                    t.wait_ge(s_v.h, v0 + n - 1)
                s_pe.inc(t.matmul(PS[n % 2][0:2, 0:512], csil[:, k, :],
                                  wtile[:, i % 2, :],
                                  start=(k == 0), stop=(k == NCH - 1)))

        @blk.vector
        def _(v):
            pe0 = s_pe.v - 12 * NCH
            for n in range(12):
                v.wait_ge(s_pe.h, pe0 + (n + 1) * NCH)
                s_v.inc(v.tensor_copy(modr[:, n * 512:(n + 1) * 512],
                                      PS[n % 2][0:2, 0:512]))

    with nc.Block() as blk:
        names = ("sa", "ca1", "ga", "sb", "cb1", "gb")

        @blk.tensor
        def _(t):
            ev0 = c.sems["s_ev"].v
            for q in range(6):
                if q >= 2:
                    t.wait_ge(c.sems["s_ev"].h, ev0 + q - 1)
                for ch in range(NCH):
                    ins = t.transpose(
                        PS[q % 2][:, ch * 2:(ch + 1) * 2],
                        modr[:, q * D + ch * 128:q * D + (ch + 1) * 128],
                        P["ident_f"][0:2, 0:2])
                s_pe.inc(ins)

        @blk.vector
        def _(v):
            pe0 = s_pe.v - 6
            s_ev = c.sems["s_ev"]
            for q, nm in enumerate(names):
                v.wait_ge(s_pe.h, pe0 + q + 1)
                s_ev.inc(v.tensor_copy(
                    P[nm][:, :].rearrange("p (ch b) -> p ch b", b=BL),
                    PS[q % 2][:, 0:2 * NCH].rearrange("p (ch b) -> p ch b",
                                                      b=BL)))

        @blk.sync
        def _(sy):
            c.sems["s_sp"].dma(sy.dma_start(out=c.io["modr_sp"].ap(),
                                            in_=modr))
            c.sems["s_sp"].wait(sy)

    with nc.Block() as blk:
        @blk.vector
        def _(v):
            _vs(c, v, v.tensor_scalar(P["ca1"][:, :], P["ca1"][:, :], 1.0,
                                      None, OP.add))
            _vs(c, v, v.tensor_scalar(P["cb1"][:, :], P["cb1"][:, :], 1.0,
                                      None, OP.add))
            sgn = AF4.view(c.f_sq, 128, NCH * BL)
            mag = AF4.view(c.f_sq + NCH * BL, 128, NCH * BL)
            for nm in ("ga", "gb"):
                g = P[nm][:, :]
                _vs(c, v, v.tensor_scalar(sgn, g, 0.0, 2.0, OP.is_ge,
                                          OP.mult))
                _vs(c, v, v.tensor_scalar(sgn, sgn, -1.0, None, OP.add))
                _vs(c, v, v.tensor_mul(mag, g, sgn))
                _vs(c, v, v.tensor_scalar(mag, mag, 0.25, None, OP.max))
                _vs(c, v, v.tensor_mul(g, sgn, mag))
            _vs(c, v, v.tensor_mul(
                P["gab"][:, :].rearrange("p (ch b) -> p ch b", b=BL),
                P["ga"][:, :].rearrange("p (ch b) -> p ch b", b=BL),
                P["bo"][:, :].unsqueeze(2).to_broadcast((128, NCH, BL))))


def _dequant(c):
    nc, P, AB, AF4 = c.nc, c.P, c.AB, c.AF
    s_ld, s_v, s_gp, s_sp = (c.sems[k] for k in
                             ("s_ld", "s_v", "s_gp", "s_sp"))
    jobs = [("wu_nat", "wu_dq", FI, D), ("wg_nat", "wg_dq", FI, D),
            ("wd_nat", "wd_dq", D, FI)]
    for nat, dqn, O, I in jobs:
        NOB = O // 128
        NIB = I // 32
        with nc.Block() as blk:
            wt = AF4.view(c.f_dq, 128, 2, I)
            sS = AF4.view(c.f_dq + 2 * I, 128, 2, NIB)
            sI = AF4.view(c.f_dq + 2 * I + 2 * NIB, 128, 2, NIB)
            dq = AB.view(c.o_wdq, 128, 2, I)

            @blk.sync
            def _(sy):
                sp0 = c.sems["s_gd"].v
                for ob in range(NOB):
                    if ob >= 2:
                        sy.wait_ge(c.sems["s_gd"].h, sp0 + (ob - 1) * 16)
                    s_ld.dma(sy.dma_start(
                        out=wt[:, ob % 2, :],
                        in_=c.io[nat].ap()[ob * 128:(ob + 1) * 128, :]))

            @blk.vector
            def _(v):
                ld0 = s_ld.v - NOB * 16
                for ob in range(NOB):
                    b = ob % 2
                    v.wait_ge(s_ld.h, ld0 + (ob + 1) * 16)
                    tile3 = wt[:, b, :].rearrange("p (nb w) -> p nb w", w=32)
                    _vs(c, v, v.tensor_reduce(sS[:, b, :], tile3, AX.X,
                                              OP.max,
                                              apply_absolute_value=True))
                    _vs(c, v, v.tensor_scalar(sS[:, b, :], sS[:, b, :],
                                              1.0 / 7.0, 1e-12, OP.mult,
                                              OP.max))
                    _vs(c, v, v.reciprocal(sI[:, b, :], sS[:, b, :]))
                    v.tensor_tensor(
                        tile3, tile3,
                        sI[:, b, :].unsqueeze(2).to_broadcast((128, NIB, 32)),
                        OP.mult)
                    v.tensor_scalar(wt[:, b, :], wt[:, b, :], MAGIC, None,
                                    OP.add)
                    v.tensor_scalar(wt[:, b, :], wt[:, b, :], -MAGIC, None,
                                    OP.add)
                    s_v.inc(v.tensor_tensor(
                        dq[:, b, :].rearrange("p (nb w) -> p nb w", w=32),
                        tile3,
                        sS[:, b, :].unsqueeze(2).to_broadcast((128, NIB, 32)),
                        OP.mult))

            @blk.gpsimd
            def _(gp):
                v0 = s_v.v - NOB
                for ob in range(NOB):
                    gp.wait_ge(s_v.h, v0 + ob + 1)
                    c.sems["s_gd"].dma(gp.dma_start(
                        out=c.io[dqn].ap()[ob * 128:(ob + 1) * 128, :],
                        in_=dq[:, ob % 2, :]))
                c.sems["s_gd"].wait(gp)


def _ln1(c, s):
    nc, P, AB, AF4, PS = c.nc, c.P, c.AB, c.AF, c.PS
    s_ld, s_v, s_a, s_pe, s_sp, s_gp = (c.sems[k] for k in
                                        ("s_ld", "s_v", "s_a", "s_pe",
                                         "s_sp", "s_gp"))
    xstg = AF4.view(c.f_xstg, 128, 2, TSL)
    sqt = AF4.view(c.f_sq, 128, TSL)
    xb = AB.view(c.o_stg, 128, 2, TSL)
    sqb = AB.view(c.o_stg + 2 * TSL, 128, 2, TSL)
    hF = AB.view(c.o_hF, 128, NCH, T)
    Pb = AF4.view(c.f_pq, 128, T)
    Qb = AF4.view(c.f_pq + T, 128, T)
    xT = c.io["xT"].ap()

    with nc.Block() as blk:
        @blk.sync
        def _(sy):
            a0 = s_a.v
            for i in range(NSL * NCH):
                if i >= 2:
                    sy.wait_ge(s_a.h, a0 + 2 * (i - 1))
                s_ld.dma(sy.dma_start(
                    out=xstg[:, i % 2, :],
                    in_=xT[s, :, :].rearrange("(ch p) t -> p ch t", p=128)
                    [:, i % NCH, (i // NCH) * TSL:(i // NCH + 1) * TSL]))

        @blk.scalar
        def _(se):
            ld0 = s_ld.v - NSL * NCH * 16
            for i in range(NSL * NCH):
                se.wait_ge(s_ld.h, ld0 + (i + 1) * 16)
                se.activation(xb[:, i % 2, :], xstg[:, i % 2, :], AF.Identity)
                s_a.inc(se.activation(sqb[:, i % 2, :], xstg[:, i % 2, :],
                                      AF.Square), 2)

        @blk.tensor
        def _(t):
            a0 = s_a.v - NSL * NCH * 2
            v0 = s_v.v
            for i in range(NSL * NCH):
                sl, ch = divmod(i, NCH)
                bk = 4 + 2 * (sl % 2)
                t.wait_ge(s_a.h, a0 + (i + 1) * 2)
                if ch == 0 and sl >= 2:
                    t.wait_ge(s_v.h, v0 + sl - 1)
                t.matmul(PS[bk][0:1, 0:TSL], P["onecol"][:, :], xb[:, i % 2, :],
                         start=(ch == 0), stop=(ch == NCH - 1))
                s_pe.inc(t.matmul(PS[bk + 1][0:1, 0:TSL], P["onecol"][:, :],
                                  sqb[:, i % 2, :],
                                  start=(ch == 0), stop=(ch == NCH - 1)))

        @blk.vector
        def _(v):
            pe0 = s_pe.v - NSL * NCH
            for sl in range(NSL):
                v.wait_ge(s_pe.h, pe0 + (sl + 1) * NCH)
                csl = slice(sl * TSL, (sl + 1) * TSL)
                bk = 4 + 2 * (sl % 2)
                mrow = Qb[0:1, csl]
                rrow = Pb[0:1, csl]
                v.tensor_scalar(mrow, PS[bk][0:1, 0:TSL], 1.0 / D, None,
                                OP.mult)
                v.tensor_scalar(rrow, PS[bk + 1][0:1, 0:TSL], 1.0 / D, None,
                                OP.mult)
                v.tensor_mul(sqt[0:1, 0:TSL], mrow, mrow)
                s_v.inc(v.tensor_sub(rrow, rrow, sqt[0:1, 0:TSL]))

        @blk.scalar
        def _(se):
            s_v.wait(se)
            row = Pb[0:1, 0:T]
            se.activation(row, row, AF.Ln, bias=P['sc'][0:1, 12:13])
            s_a.inc(se.activation(row, row, AF.Exp, scale=-0.5))

        @blk.vector
        def _(v):
            s_a.wait(v)
            s_v.inc(v.scalar_tensor_tensor(Qb[0:1, 0:T], Qb[0:1, 0:T], -1.0,
                                           Pb[0:1, 0:T], OP.mult, OP.mult))

        @blk.gpsimd
        def _(gp):
            s_gd = c.sems["s_gd"]
            s_v.wait(gp)
            s_gd.dma(gp.dma_start(out=c.io["bc_sp"].ap()[0:1, :],
                                  in_=Qb[0:1, 0:T]))
            s_gd.dma(gp.dma_start(out=c.io["bc_sp"].ap()[1:2, :],
                                  in_=Pb[0:1, 0:T]))
            gp.wait_ge(s_gd.h, s_gd.v)
            s_gd.dma(gp.dma_start(
                out=Qb, in_=c.io["bc_sp"].ap()[0, :].partition_broadcast(128)))
            s_gd.dma(gp.dma_start(
                out=Pb, in_=c.io["bc_sp"].ap()[1, :].partition_broadcast(128)))
            s_gd.wait(gp)

    with nc.Block() as blk:
        @blk.sync
        def _(sy):
            v0 = c.sems["s_v"].v
            for i in range(NSL * NCH):
                if i >= 2:
                    sy.wait_ge(c.sems["s_v"].h, v0 + i - 1)
                s_ld.dma(sy.dma_start(
                    out=xstg[:, i % 2, :],
                    in_=xT[s, :, :].rearrange("(ch p) t -> p ch t", p=128)
                    [:, i % NCH, (i // NCH) * TSL:(i // NCH + 1) * TSL]))

        @blk.vector
        def _(v):
            ld0 = s_ld.v - NSL * NCH * 16
            for i in range(NSL * NCH):
                sl, ch = divmod(i, NCH)
                csl = slice(sl * TSL, (sl + 1) * TSL)
                v.wait_ge(s_ld.h, ld0 + (i + 1) * 16)
                v.tensor_mul(sqt, xstg[:, i % 2, :], Pb[:, csl])
                v.tensor_add(sqt, sqt, Qb[:, csl])
                s_v.inc(v.tensor_scalar(
                    hF[:, ch, csl], sqt,
                    P["ca1"][:, ch * BL + s:ch * BL + s + 1],
                    P["sa"][:, ch * BL + s:ch * BL + s + 1],
                    OP.mult, OP.add))


def _qkv(c, s):
    nc, P, AB, AF4, PS = c.nc, c.P, c.AB, c.AF, c.PS
    s_w, s_pe, s_ev, s_sp, s_ld = (c.sems[k] for k in
                                   ("s_w", "s_pe", "s_ev", "s_sp", "s_ld"))
    hF = AB.view(c.o_hF, 128, NCH, T)
    wqkv = AB.view(c.o_wqkv, 128, NCH, 3 * D)
    vT = AB.view(c.o_vT, 128, 16, D)
    stg = AB.view(c.o_stg, 128, 2, 3 * D)
    qkv_sp = c.io["qkv_sp"].ap()

    with nc.Block() as blk:
        @blk.gpsimd
        def _(gp):
            for ch in range(NCH):
                s_w.dma(gp.dma_start(
                    out=wqkv[:, ch, 0:D],
                    in_=c.io["w_q"].ap()[ch * 128:(ch + 1) * 128, :]))
                s_w.dma(gp.dma_start(
                    out=wqkv[:, ch, D:3 * D],
                    in_=c.io["w_kv"].ap()[ch * 128:(ch + 1) * 128, :]))

        @blk.tensor
        def _(t):
            s_w.wait(t)
            ev0 = s_ev.v
            NT = T // 128
            for tt in range(NT):
                for nn in range(6):
                    j = tt * 6 + nn
                    if j >= 4:
                        t.wait_ge(s_ev.h, ev0 + j - 3)
                    for ch in range(NCH):
                        ins = t.matmul(PS[j % 4][:, 0:512],
                                       hF[:, ch, tt * 128:(tt + 1) * 128],
                                       wqkv[:, ch, nn * 512:(nn + 1) * 512],
                                       start=(ch == 0), stop=(ch == NCH - 1))
                    s_pe.inc(ins)

        @blk.vector
        def _(v):
            pe0 = s_pe.v - 6 * (T // 128)
            sp0 = c.sems["s_gd"].v
            for tt in range(T // 128):
                if tt >= 2:
                    v.wait_ge(c.sems["s_gd"].h, sp0 + (tt - 1) * 16)
                b = tt % 2
                for nn in range(6):
                    j = tt * 6 + nn
                    v.wait_ge(s_pe.h, pe0 + j + 1)
                    s_ev.inc(v.tensor_copy(stg[:, b, nn * 512:(nn + 1) * 512],
                                           PS[j % 4][:, 0:512]))
                v.tensor_copy(vT[:, tt, :], stg[:, b, 2 * D:3 * D])

        @blk.gpsimd
        def _(gp):
            s_gd = c.sems["s_gd"]
            ev0 = s_ev.v - 6 * (T // 128)
            for tt in range(T // 128):
                gp.wait_ge(s_ev.h, ev0 + (tt + 1) * 6)
                s_gd.dma(gp.dma_start(
                    out=qkv_sp[s, tt * 128:(tt + 1) * 128, :],
                    in_=stg[:, tt % 2, :]))
            s_gd.wait(gp)


def _tload_qk(c, s, heads):
    nc, AB = c.nc, c.AB
    s_tl = c.sems["s_tl"]
    qkv_sp = c.io["qkv_sp"].ap()
    with nc.Block() as blk:
        @blk.scalar
        def _(se):
            for hp in range(2):
                h0 = heads[2 * hp]
                for (dst_off, fbase) in ((c.o_qF, 0), (c.o_kF, D)):
                    dst = AB.t[:, dst_off + hp * T:dst_off + (hp + 1) * T]
                    s_tl.dma(se.dma_start_transpose(
                        dst, qkv_sp[s, :, fbase + h0 * 64:fbase + h0 * 64 + 128]))
            s_tl.wait(se)
            if c.dbg:
                ti = getattr(c, "tl_i", 0)
                c.tl_i = ti + 1
                tl = c.io["tl_sp"].ap()
                s_tl.dma(se.dma_start(
                    out=tl[s, ti % 4, 0, :, :],
                    in_=AB.t[:, c.o_qF:c.o_qF + 2 * T]))
                s_tl.dma(se.dma_start(
                    out=tl[s, ti % 4, 1, :, :],
                    in_=AB.t[:, c.o_kF:c.o_kF + 2 * T]))
                s_tl.wait(se)


def _local(c, s):
    nc, P, AB, AF4, PS = c.nc, c.P, c.AB, c.AF, c.PS
    s_pe, s_v, s_a, s_ev = (c.sems[k] for k in ("s_pe", "s_v", "s_a", "s_ev"))
    _tload_qk(c, s, [0, 1, 2, 3])
    att = AF4.view(c.f_sq, 128, 192)
    att2 = AF4.view(c.f_att, 128, 3, 128)
    nmx = AF4.view(c.f_sm, 128, 1)
    sumc = AF4.view(c.f_sm + 1, 128, 1)
    rc = AF4.view(c.f_sm + 2, 128, 1)
    a2t = AB.view(c.o_a2t, 128, 8, 128)
    aoF = AB.view(c.o_ao, 128, NCH, T)
    vT = AB.view(c.o_vT, 128, 16, D)

    iters = LOCAL_ITERS or [(h, g) for h in range(4) for g in range(16)]

    def chunks_of(g):
        w0 = min(max(128 * g - 32, 0), T - 192)
        out = []
        t0 = w0
        while t0 < w0 + 192:
            t1 = min((t0 // 128 + 1) * 128, w0 + 192)
            out.append((t0 - w0, t0 // 128, t0 % 128, t1 - t0))
            t0 = t1
        return w0, out

    it_groups = ([[x] for x in iters] if LOCAL_BLOCKS else [iters])
    for iters in it_groups:
      with nc.Block() as blk:
        pe0, v0, a0, ev0 = s_pe.v, s_v.v, s_a.v, s_ev.v

        @blk.tensor
        def _(t):
            for it, (h, g) in enumerate(iters):
                hp, half = h // 2, h % 2
                b = half * 64
                w0, chs = chunks_of(g)
                qs = AB.t[b:b + 64, c.o_qF + hp * T + g * 128:
                          c.o_qF + hp * T + (g + 1) * 128]
                ks = AB.t[b:b + 64, c.o_kF + hp * T + w0:
                          c.o_kF + hp * T + w0 + 192]
                dbank = PS[it % 2]
                if SERIAL and it >= 1:
                    t.wait_ge(s_ev.h, ev0 + it)
                if it >= 2:
                    t.wait_ge(s_v.h, v0 + (it - 2) * 3 + 1)
                s_pe.inc(t.matmul(dbank[:, 0:192], qs, ks,
                                  start=True, stop=True))
                t.wait_ge(s_v.h, v0 + it * 3 + 2)
                tb = PS[2 + it % 2]
                ins = None
                for ci in range(len(chs)):
                    ins = t.transpose(tb[:, ci * 128:(ci + 1) * 128],
                                      att2[:, ci, :], P["ident_f"][:, :])
                s_pe.inc(ins)
                t.wait_ge(s_v.h, v0 + it * 3 + 3)
                if it >= 2:
                    t.wait_ge(s_ev.h, ev0 + it - 1)
                ob = PS[4 + it % 2]
                slot0 = (it % 2) * 3
                first = True
                for ci, (j0, vt, rb, ln) in enumerate(chs):
                    lhs = vT[:, vt, h * 64:(h + 1) * 64]
                    rhs = a2t[:, slot0 + ci, :]
                    ins = t.matmul(ob[b:b + 64, 0:128], lhs, rhs,
                                   start=first, stop=(ci == len(chs) - 1),
                                   tile_position=(0, b))
                    first = False
                s_pe.inc(ins)

        @blk.vector
        def _(v):
            for it, (h, g) in enumerate(iters):
                hp, half = h // 2, h % 2
                b = half * 64
                dbank = PS[it % 2]
                v.wait_ge(s_pe.h, pe0 + it * 3 + 1)
                mi = 0 if g == 0 else (2 if g == 15 else 1)
                v.tensor_add(att[:, 0:192], dbank[:, 0:192],
                             P["mask_loc"][:, mi * 192:(mi + 1) * 192])
                s_v.inc(v.tensor_reduce(nmx, att[:, 0:192], AX.X, OP.max,
                                        negate=True))
                v.wait_ge(s_a.h, a0 + it + 1)
                _vs(c, v, v.reciprocal(rc, sumc))
                w0, chs = chunks_of(g)
                if g in (0, 1, 15):
                    v.memset(att2[:, :, :], 0.0)
                ins = None
                for ci, (j0, vt, rb, ln) in enumerate(chs):
                    ins = v.tensor_scalar(att2[:, ci, rb:rb + ln],
                                          att[:, j0:j0 + ln], rc[:, 0:1],
                                          None, OP.mult)
                s_v.inc(ins)
                v.wait_ge(s_pe.h, pe0 + it * 3 + 2)
                tb = PS[2 + it % 2]
                slot0 = (it % 2) * 3
                ins = None
                for ci in range(len(chs)):
                    ins = v.tensor_copy(a2t[:, slot0 + ci, :],
                                        tb[:, ci * 128:(ci + 1) * 128])
                s_v.inc(ins)
                v.wait_ge(s_pe.h, pe0 + it * 3 + 3)
                ob = PS[4 + it % 2]
                s_ev.inc(v.tensor_copy(aoF[b:b + 64, h // 2, g * 128:(g + 1) * 128],
                                       ob[b:b + 64, 0:128]))

        @blk.scalar
        def _(se):
            for it, (h, g) in enumerate(iters):
                se.wait_ge(s_v.h, v0 + it * 3 + 1)
                s_a.inc(se.activation(att[:, 0:192], att[:, 0:192], AF.Exp,
                                      bias=nmx[:, 0:1],
                                      accum_out=sumc[:, 0:1]))


def _sink(c, s):
    nc, P, AB, AF4, PS = c.nc, c.P, c.AB, c.AF, c.PS
    s_pe, s_v, s_a, s_ev, s_gd, s_gp = (
        c.sems[k] for k in ("s_pe", "s_v", "s_a", "s_ev", "s_gd", "s_gp"))
    qkv_sp = c.io["qkv_sp"].ap()
    bvr_sp = c.io["bvr_sp"].ap()
    att = AF4.view(c.f_sq, 128, 256)
    nmx = AF4.view(c.f_sm, 128, 1)
    nms = AF4.view(c.f_sm + 1, 128, 1)
    sumc = AF4.view(c.f_sm + 2, 128, 1)
    rc = AF4.view(c.f_sm + 3, 128, 1)
    rfl = AF4.view(c.f_sm + 4, 128, 64)
    a2t = AB.view(c.o_a2t, 128, 8, 128)
    aoF = AB.view(c.o_ao, 128, NCH, T)
    vT = AB.view(c.o_vT, 128, 16, D)
    o_fk = c.o_asm

    for g4 in range(3):
        heads = [4 + g4 * 4 + i for i in range(4)]
        _tload_qk(c, s, heads)
        with nc.Block() as blk:
            @blk.gpsimd
            def _(gp):
                kbm = c.AB.t[0:64, c.o_kbm:c.o_kbm + 8192]
                vbm = c.AB.t[0:64, c.o_vbm:c.o_vbm + 8192]
                src = qkv_sp[s, :, :].rearrange(
                    "(v tp) f -> v tp f", tp=32)
                s_gd.dma(gp.dma_start(
                    out=kbm.rearrange("p (tp hh e) -> p tp hh e", hh=4, e=64),
                    in_=src[:, :, D + heads[0] * 64:D + (heads[3] + 1) * 64]
                    .rearrange("v tp (hh e) -> v tp hh e", e=64)))
                s_gd.dma(gp.dma_start(
                    out=vbm.rearrange("p (tp hh e) -> p tp hh e", hh=4, e=64),
                    in_=src[:, :, 2 * D + heads[0] * 64:2 * D + (heads[3] + 1) * 64]
                    .rearrange("v tp (hh e) -> v tp hh e", e=64)))
                s_gd.wait(gp)

        for hh in range(4):
            h = heads[hh]
            hp, half = hh // 2, hh % 2
            b = half * 64
            ch = h // 2
            with nc.Block() as blk:
                qs3 = c.AB.t[b:b + 64, c.o_qF + hp * T:c.o_qF + (hp + 1) * T] \
                    .rearrange("p (u w) -> p u w", w=32)
                ks3 = c.AB.t[b:b + 64, c.o_kF + hp * T:c.o_kF + (hp + 1) * T] \
                    .rearrange("p (u w) -> p u w", w=32)
                qsum = c.AB.t[b:b + 64, o_fk:o_fk + 64]
                ksum = c.AB.t[b:b + 64, o_fk + 64:o_fk + 128]
                RT0 = c.AB.t[0:64, o_fk + 192:o_fk + 256]
                kbm = c.AB.t[0:64, c.o_kbm:c.o_kbm + 8192]
                vbm = c.AB.t[0:64, c.o_vbm:c.o_vbm + 8192]
                bvrg = c.AB.t[0:64, c.o_bvrg:c.o_bvrg + 8192]

                @blk.vector
                def _(v):
                    with nc.allow_low_precision(
                            reason="bf16 matmul operand rounding (feats)"):
                        v.tensor_reduce(qsum, qs3, AX.X, OP.add)
                        s_v.inc(v.tensor_reduce(ksum, ks3, AX.X, OP.add))

                @blk.tensor
                def _(t):
                    s_v.wait(t)
                    psR = PS[6]
                    t.matmul(psR[0:64, 0:64], qsum,
                             P["sw"][b:b + 64, :].rearrange(
                                 "p (hh q e) -> p hh q e", q=2, e=64)
                             [:, h - 4, 0, :], start=True, stop=False)
                    s_pe.inc(t.matmul(psR[0:64, 0:64], ksum,
                                      P["sw"][b:b + 64, :].rearrange(
                                          "p (hh q e) -> p hh q e", q=2, e=64)
                                      [:, h - 4, 1, :],
                                      start=False, stop=True))

                @blk.vector
                def _(v):
                    s_pe.wait(v)
                    psR = PS[6]
                    _vs(c, v, v.tensor_scalar(rfl[0:64, :], psR[0:64, 0:64],
                                              0.0, None, OP.max))
                    s_v.inc(v.tensor_reduce(nmx[0:64, :], rfl[0:64, :], AX.X,
                                            OP.max, negate=True))

                @blk.scalar
                def _(se):
                    s_v.wait(se)
                    s_a.inc(se.activation(rfl[0:64, :], rfl[0:64, :], AF.Exp,
                                          bias=nmx[0:64, 0:1],
                                          accum_out=sumc[0:64, 0:1]))

                @blk.vector
                def _(v):
                    s_a.wait(v)
                    _vs(c, v, v.reciprocal(rc[0:64, :], sumc[0:64, :]))
                    s_v.inc(v.tensor_scalar(rfl[0:64, :], rfl[0:64, :],
                                            rc[0:64, 0:1], None, OP.mult))

                @blk.tensor
                def _(t):
                    s_v.wait(t)
                    psT = PS[7]
                    s_pe.inc(t.transpose(psT[0:64, 0:64], rfl[0:64, :],
                                         P["ident_f"][0:64, 0:64]))

                @blk.vector
                def _(v):
                    s_pe.wait(v)
                    psT = PS[7]
                    s_v.inc(v.tensor_copy(RT0, psT[0:64, 0:64]))

                @blk.tensor
                def _(t):
                    s_v.wait(t)
                    for bank in range(4):
                        for tp8 in range(8):
                            tp = bank * 8 + tp8
                            ins = t.matmul(
                                PS[bank][b:b + 64, tp8 * 64:(tp8 + 1) * 64],
                                kbm[:, tp * 256 + hh * 64:tp * 256 + (hh + 1) * 64],
                                RT0, start=True, stop=True,
                                tile_position=(0, b))
                        s_pe.inc(ins)
                    for nchk in range(4):
                        s_pe.inc(t.matmul(
                            PS[4 + nchk % 2][0:64, 0:512], RT0,
                            vbm[:, :].rearrange("p (tp hh2 e) -> p tp hh2 e",
                                                hh2=4, e=64)
                            [:, nchk * 8:(nchk + 1) * 8, hh, :],
                            start=True, stop=True))
                        if nchk >= 1:
                            t.wait_ge(s_v.h, s_v.v + 4 + nchk)

                @blk.vector
                def _(v):
                    pe_b = s_pe.v - 8
                    bkrF = c.AB.t[b:b + 64, c.o_bkrF + hp * T:
                                  c.o_bkrF + (hp + 1) * T]
                    for bank in range(4):
                        v.wait_ge(s_pe.h, pe_b + bank + 1)
                        s_v.inc(v.tensor_copy(
                            bkrF.rearrange("p (u w) -> p u w", w=32)
                            [:, :, bank * 8:(bank + 1) * 8]
                            .transpose([0, 2, 1]),
                            PS[bank][b:b + 64, 0:512].rearrange(
                                "p (tp u) -> p tp u", u=64)))
                    for nchk in range(4):
                        v.wait_ge(s_pe.h, pe_b + 4 + nchk + 1)
                        s_v.inc(v.tensor_copy(
                            bvrg[:, :].rearrange("p (tp hh2 e) -> p tp hh2 e",
                                                 hh2=4, e=64)
                            [:, nchk * 8:(nchk + 1) * 8, hh, :],
                            PS[4 + nchk % 2][0:64, 0:512].rearrange(
                                "p (tp e) -> p tp e", e=64)))

        with nc.Block() as blk:
            @blk.gpsimd
            def _(gp):
                bvrg = c.AB.t[0:64, c.o_bvrg:c.o_bvrg + 8192]
                s_gd.dma(gp.dma_start(out=bvr_sp[s, g4, :, :], in_=bvrg))
                s_gd.wait(gp)
                bvrt = c.AB.t[:, c.o_bvrt:c.o_bvrt + 4096].rearrange(
                    "p (pk ce) -> p pk ce", ce=256)
                s_gd.dma(gp.dma_start(
                    out=bvrt,
                    in_=bvr_sp[s, g4, :, :].rearrange(
                        "(pk us) (tp ce) -> (us tp) pk ce", us=4, ce=256)))
                s_gd.wait(gp)

        with nc.Block() as blk:
            pe0, v0, a0, ev0 = s_pe.v, s_v.v, s_a.v, s_ev.v
            iters = [(hh, pk) for hh in range(4) for pk in range(16)]

            @blk.tensor
            def _(t):
                for it, (hh, pk) in enumerate(iters):
                    h = heads[hh]
                    hp, half = hh // 2, hh % 2
                    b = half * 64
                    qs = c.AB.t[b:b + 64, c.o_qF + hp * T + pk * 128:
                                c.o_qF + hp * T + (pk + 1) * 128]
                    bkr = c.AB.t[b:b + 64, c.o_bkrF + hp * T + pk * 128:
                                 c.o_bkrF + hp * T + (pk + 1) * 128]
                    ks = c.AB.t[b:b + 64, c.o_kF + hp * T + pk * 128:
                                c.o_kF + hp * T + (pk + 1) * 128]
                    dbank = PS[it % 2]
                    if it >= 2:
                        t.wait_ge(s_v.h, v0 + (it - 2) * 3 + 1)
                    t.matmul(dbank[:, 0:128], qs, bkr, start=True, stop=False)
                    s_pe.inc(t.matmul(dbank[:, 128:256], qs, ks,
                                      start=False, stop=True))
                    t.wait_ge(s_v.h, v0 + it * 3 + 2)
                    tb = PS[2 + it % 2]
                    t.transpose(tb[:, 0:128], att[:, 0:128], P["ident_f"][:, :])
                    s_pe.inc(t.transpose(tb[:, 128:256], att[:, 128:256],
                                         P["ident_f"][:, :]))
                    t.wait_ge(s_v.h, v0 + it * 3 + 3)
                    if it >= 2:
                        t.wait_ge(s_ev.h, ev0 + it - 1)
                    ob = PS[4 + it % 2]
                    bvrt = c.AB.t[:, c.o_bvrt + pk * 256 + hh * 64:
                                  c.o_bvrt + pk * 256 + (hh + 1) * 64]
                    t.matmul(ob[b:b + 64, 0:128], bvrt,
                             a2t[:, 6, :], start=True, stop=False,
                             tile_position=(0, b))
                    s_pe.inc(t.matmul(ob[b:b + 64, 0:128],
                                      vT[:, pk, h * 64:(h + 1) * 64],
                                      a2t[:, 7, :], start=False, stop=True,
                                      tile_position=(0, b)))

            @blk.vector
            def _(v):
                for it, (hh, pk) in enumerate(iters):
                    h = heads[hh]
                    half = hh % 2
                    b = half * 64
                    dbank = PS[it % 2]
                    v.wait_ge(s_pe.h, pe0 + it * 3 + 1)
                    v.tensor_add(att[:, 0:128], dbank[:, 0:128],
                                 P["mask_snk"][:, 0:128])
                    v.tensor_add(att[:, 128:256], dbank[:, 128:256],
                                 P["mask_snk"][:, 128:256])
                    _vs(c, v, v.tensor_reduce(nmx, att[:, 0:256], AX.X,
                                              OP.max, negate=True))
                    s_v.inc(v.tensor_scalar(nms, nmx, 0.125, None, OP.mult))
                    v.wait_ge(s_a.h, a0 + it + 1)
                    _vs(c, v, v.reciprocal(rc, sumc))
                    s_v.inc(v.tensor_scalar(att[:, 0:256], att[:, 0:256],
                                            rc[:, 0:1], None, OP.mult))
                    v.wait_ge(s_pe.h, pe0 + it * 3 + 2)
                    tb = PS[2 + it % 2]
                    s_v.inc(v.tensor_copy(a2t[:, 6:8, :].rearrange(
                        "p a b -> p (a b)"), tb[:, 0:256]))
                    v.wait_ge(s_pe.h, pe0 + it * 3 + 3)
                    ob = PS[4 + it % 2]
                    s_ev.inc(v.tensor_copy(
                        aoF[b:b + 64, h // 2, pk * 128:(pk + 1) * 128],
                        ob[b:b + 64, 0:128]))

            @blk.scalar
            def _(se):
                for it, (hh, pk) in enumerate(iters):
                    se.wait_ge(s_v.h, v0 + it * 3 + 1)
                    s_a.inc(se.activation(att[:, 0:256], att[:, 0:256],
                                          AF.Exp, bias=nms[:, 0:1],
                                          scale=0.125,
                                          accum_out=sumc[:, 0:1]))


def _outproj(c, s):
    nc, P, AB, AF4, PS = c.nc, c.P, c.AB, c.AF, c.PS
    s_w, s_pe, s_v, s_a, s_ld, s_gd, s_ev = (
        c.sems[k] for k in ("s_w", "s_pe", "s_v", "s_a", "s_ld", "s_gd",
                            "s_ev"))
    aoF = AB.view(c.o_ao, 128, NCH, T)
    wo = AB.view(c.o_wo, 128, NCH, D)
    xstg = AF4.view(c.f_xstg, 128, 2, TSL)
    x2t = AF4.view(c.f_x2t, 128, 2, TSL)
    sqt = AF4.view(c.f_sq, 128, TSL)
    x2b = AB.view(c.o_stg, 128, 2, TSL)
    sqb = AB.view(c.o_stg + 2 * TSL, 128, 2, TSL)
    Pb = AF4.view(c.f_pq, 128, T)
    Qb = AF4.view(c.f_pq + T, 128, T)
    xT = c.io["xT"].ap()
    x2_sp = c.io["x2_sp"].ap()

    with nc.Block() as blk:
        @blk.gpsimd
        def _(gp):
            for ci in range(NCH):
                c.sems["s_gd"].dma(gp.dma_start(
                    out=wo[:, ci, :],
                    in_=c.io["w_out"].ap()[ci * 128:(ci + 1) * 128, :]))
            if c.dbg:
                c.sems["s_gd"].dma(gp.dma_start(
                    out=c.io["ao_sp"].ap()[s, :, :],
                    in_=aoF[:, :, :].rearrange("p ch t -> p (ch t)")))
            c.sems["s_gd"].wait(gp)

    with nc.Block() as blk:
        pe0, v0, a0, ev0, ld0, gd0 = (s_pe.v, s_v.v, s_a.v, s_ev.v, s_ld.v,
                                      s_gd.v)
        iters = [(sl, mo) for sl in range(NSL) for mo in range(NCH)]

        @blk.sync
        def _(sy):
            for it, (sl, mo) in enumerate(iters):
                if it >= 2:
                    sy.wait_ge(s_v.h, v0 + (it - 1) * 3)
                s_ld.dma(sy.dma_start(
                    out=xstg[:, it % 2, :],
                    in_=xT[s, :, :].rearrange("(ch p) t -> p ch t", p=128)
                    [:, mo, sl * TSL:(sl + 1) * TSL]))

        @blk.tensor
        def _(t):
            for it, (sl, mo) in enumerate(iters):
                bank = PS[it % 4]
                if it >= 3:
                    t.wait_ge(s_v.h, v0 + (it - 3) * 3 + 1)
                for ci in range(NCH):
                    ins = t.matmul(bank[:, 0:TSL],
                                   wo[:, ci, mo * 128:(mo + 1) * 128],
                                   aoF[:, ci, sl * TSL:(sl + 1) * TSL],
                                   start=(ci == 0), stop=(ci == NCH - 1))
                s_pe.inc(ins)
                t.wait_ge(s_a.h, a0 + it + 1)
                sb = 4 + 2 * (sl % 2)
                if mo == 0 and sl >= 2:
                    t.wait_ge(s_v.h, v0 + (sl - 1) * NCH * 3)
                t.matmul(PS[sb][0:1, 0:TSL], P["onecol"][:, :],
                         x2b[:, it % 2, :], start=(mo == 0),
                         stop=(mo == NCH - 1))
                s_pe.inc(t.matmul(PS[sb + 1][0:1, 0:TSL], P["onecol"][:, :],
                                  sqb[:, it % 2, :], start=(mo == 0),
                                  stop=(mo == NCH - 1)))

        @blk.vector
        def _(v):
            for it, (sl, mo) in enumerate(iters):
                bank = PS[it % 4]
                v.wait_ge(s_pe.h, pe0 + it * 2 + 1)
                v.wait_ge(s_ld.h, ld0 + (it + 1) * 16)
                if it >= 2:
                    v.wait_ge(s_gd.h, gd0 + (it - 1) * 16)
                cb = mo * BL + s
                v.tensor_scalar(sqt, bank[:, 0:TSL], P["bo"][:, mo:mo + 1],
                                P["ga"][:, cb:cb + 1], OP.add, OP.mult)
                s_v.inc(v.tensor_add(x2t[:, it % 2, :], sqt,
                                     xstg[:, it % 2, :]))
                s_v.inc(v.tensor_copy(x2b[:, it % 2, :], x2t[:, it % 2, :]))
                if mo == NCH - 1:
                    v.wait_ge(s_pe.h, pe0 + (it + 1) * 2)
                    sb = 4 + 2 * (sl % 2)
                    csl = slice(sl * TSL, (sl + 1) * TSL)
                    v.tensor_scalar(Qb[0:1, csl], PS[sb][0:1, 0:TSL],
                                    1.0 / D, None, OP.mult)
                    v.tensor_scalar(Pb[0:1, csl], PS[sb + 1][0:1, 0:TSL],
                                    1.0 / D, None, OP.mult)
                    v.tensor_mul(sqt[0:1, 0:TSL], Qb[0:1, csl], Qb[0:1, csl])
                    s_v.inc(v.tensor_sub(Pb[0:1, csl], Pb[0:1, csl],
                                         sqt[0:1, 0:TSL]))
                else:
                    s_v.inc(v.engine_nop())

        @blk.scalar
        def _(se):
            for it, (sl, mo) in enumerate(iters):
                se.wait_ge(s_v.h, v0 + it * 3 + 2)
                s_a.inc(se.activation(sqb[:, it % 2, :], x2t[:, it % 2, :],
                                      AF.Square))

        @blk.gpsimd
        def _(gp):
            for it, (sl, mo) in enumerate(iters):
                gp.wait_ge(s_v.h, v0 + it * 3 + 1)
                s_gd.dma(gp.dma_start(
                    out=x2_sp[s, mo, :, sl * TSL:(sl + 1) * TSL],
                    in_=x2t[:, it % 2, :]))
            s_gd.wait(gp)

    with nc.Block() as blk:
        @blk.scalar
        def _(se):
            row = Pb[0:1, 0:T]
            se.activation(row, row, AF.Ln, bias=P["sc"][0:1, 12:13])
            s_a.inc(se.activation(row, row, AF.Exp, scale=-0.5))

        @blk.vector
        def _(v):
            s_a.wait(v)
            s_v.inc(v.scalar_tensor_tensor(Qb[0:1, 0:T], Qb[0:1, 0:T], -1.0,
                                           Pb[0:1, 0:T], OP.mult, OP.mult))

        @blk.gpsimd
        def _(gp):
            s_gd = c.sems["s_gd"]
            s_v.wait(gp)
            s_gd.dma(gp.dma_start(out=c.io["bc_sp"].ap()[2:3, :],
                                  in_=Qb[0:1, 0:T]))
            s_gd.dma(gp.dma_start(out=c.io["bc_sp"].ap()[3:4, :],
                                  in_=Pb[0:1, 0:T]))
            gp.wait_ge(s_gd.h, s_gd.v)
            s_gd.dma(gp.dma_start(
                out=Qb, in_=c.io["bc_sp"].ap()[2, :].partition_broadcast(128)))
            s_gd.dma(gp.dma_start(
                out=Pb, in_=c.io["bc_sp"].ap()[3, :].partition_broadcast(128)))
            s_gd.wait(gp)


def _h2max(c, s):
    nc, P, AB, AF4, PS = c.nc, c.P, c.AB, c.AF, c.PS
    s_ld, s_v, s_gd = (c.sems[k] for k in ("s_ld", "s_v", "s_gd"))
    xstg = AF4.view(c.f_xstg, 128, 2, TSL)
    sqt = AF4.view(c.f_sq, 128, TSL)
    tmp1 = AF4.view(c.f_sm, 128, 1)
    h2b = AB.view(c.o_stg, 128, 2, TSL)
    Pb = AF4.view(c.f_pq, 128, T)
    Qb = AF4.view(c.f_pq + T, 128, T)
    x2_sp = c.io["x2_sp"].ap()
    h2_sp = c.io["h2_sp"].ap()
    with nc.Block() as blk:
        ld0, v0, gd0 = s_ld.v, s_v.v, s_gd.v
        iters = [(sl, ch) for sl in range(NSL) for ch in range(NCH)]

        @blk.sync
        def _(sy):
            for it, (sl, ch) in enumerate(iters):
                if it >= 2:
                    sy.wait_ge(s_v.h, v0 + (it - 1) * 2)
                s_ld.dma(sy.dma_start(
                    out=xstg[:, it % 2, :],
                    in_=x2_sp[s, ch, :, sl * TSL:(sl + 1) * TSL]))

        @blk.vector
        def _(v):
            for it, (sl, ch) in enumerate(iters):
                v.wait_ge(s_ld.h, ld0 + (it + 1) * 16)
                if it >= 2:
                    v.wait_ge(s_gd.h, gd0 + (it - 1) * 16)
                csl = slice(sl * TSL, (sl + 1) * TSL)
                cb = ch * BL + s
                v.tensor_mul(sqt, xstg[:, it % 2, :], Pb[:, csl])
                v.tensor_add(sqt, sqt, Qb[:, csl])
                s_v.inc(v.tensor_scalar(
                    h2b[:, it % 2, :], sqt, P["cb1"][:, cb:cb + 1],
                    P["sb"][:, cb:cb + 1], OP.mult, OP.add))
                mc = s * len(iters) + it
                ins = v.tensor_reduce(P["mxt"][:, mc:mc + 1],
                                      h2b[:, it % 2, :], AX.X, OP.max,
                                      apply_absolute_value=True)
                s_v.inc(ins)
                if s == 1 and it == len(iters) - 1:
                    _vs(c, v, v.memset(P["sc"][:, 13:14], 0.0))
                    _vs(c, v, v.tensor_reduce(
                        P["mxc"][:, 0:1], P["mxt"][:, 0:2 * len(iters)],
                        AX.X, OP.max))

        @blk.gpsimd
        def _(gp):
            for it, (sl, ch) in enumerate(iters):
                gp.wait_ge(s_v.h, v0 + it * 2 + 1)
                s_gd.dma(gp.dma_start(
                    out=h2_sp[s, ch, :, sl * TSL:(sl + 1) * TSL],
                    in_=h2b[:, it % 2, :]))
            s_gd.wait(gp)


def _ar_scale(c, idx):
    nc, P, AF4, PS = c.nc, c.P, c.AF, c.PS
    s_pe, s_v, s_gd, s_cc = (c.sems[k] for k in
                             ("s_pe", "s_v", "s_gd", "s_cc"))
    SC = c.SC
    ar_in, ar_out = c.ar[idx]
    cols = {1: [0], 2: [1, 2], 3: [3], 4: [4]}[idx]
    k = len(cols)
    scr = AF4.view(c.f_sm + 8, 8, 1)
    with nc.Block() as blk:
        @blk.tensor
        def _(t):
            s_pe.inc(t.transpose(PS[0][0:k, 0:128],
                                 P["mxc"][:, cols[0]:cols[0] + k],
                                 P["ident_f"][:, :]))

        @blk.vector
        def _(v):
            v.memset(scr, 0.0)
            s_pe.wait(v)
            s_v.inc(v.tensor_reduce(scr[0:k, 0:1], PS[0][0:k, 0:128], AX.X,
                                    OP.max))

        @blk.gpsimd
        def _(gp):
            s_v.wait(gp)
            s_gd.dma(gp.dma_start(out=ar_in.ap()[0:1, 0:8],
                                  in_=scr[0:8, 0:1]))
            s_gd.wait(gp)
            s_cc.inc(gp.collective_compute(
                "AllReduce", OP.max, replica_groups=[list(range(RG))],
                ins=[ar_in.ap()], outs=[ar_out.ap()]))
            s_cc.wait(gp)
            s_gd.dma(gp.dma_start(
                out=P["sc"][:, 16:16 + k],
                in_=ar_out.ap()[0, 0:k].partition_broadcast(128)))
            s_gd.wait(gp)

    sc = P["sc"]
    with nc.Block() as blk:
        @blk.vector
        def _(v):
            g0 = sc[:, 16:17]
            if idx == 1:
                _vs(c, v, v.tensor_scalar(sc[:, SC["SX"]:SC["SX"] + 1], g0,
                                          1.0 / QMAX, 1e-12, OP.mult,
                                          OP.max))
                s_v.inc(v.reciprocal(
                    sc[:, SC["SXI"]:SC["SXI"] + 1],
                    sc[:, SC["SX"]:SC["SX"] + 1]))
                _vs(c, v, v.memset(sc[:, 13:14], 0.0))
            elif idx == 2:
                for j, (smax, qcol) in enumerate(
                        ((SC["S2U"], SC["QU"]), (SC["S2G"], SC["QG"]))):
                    gj = sc[:, 16 + j:17 + j]
                    _vs(c, v, v.tensor_scalar(sc[:, 20:21], gj,
                                              sc[:, SC["SX"]:SC["SX"] + 1],
                                              None, OP.mult))
                    _vs(c, v, v.tensor_scalar(sc[:, smax:smax + 1],
                                              sc[:, 20:21], 1.0 / QMAX,
                                              1e-12, OP.mult, OP.max))
                    _vs(c, v, v.reciprocal(sc[:, 21:22],
                                           sc[:, smax:smax + 1]))
                    s_v.inc(v.tensor_mul(
                        sc[:, qcol:qcol + 1],
                        sc[:, SC["SX"]:SC["SX"] + 1], sc[:, 21:22]))
                    _vs(c, v, v.memset(sc[:, 13:14], 0.0))
            elif idx == 3:
                _vs(c, v, v.tensor_scalar(sc[:, 22:23],
                                          sc[:, SC["S2G"]:SC["S2G"] + 1],
                                          1.0 / 65536.0, None, OP.mult))
                _vs(c, v, v.tensor_mul(sc[:, 20:21], g0, sc[:, 22:23]))
                _vs(c, v, v.tensor_scalar(sc[:, SC["S3"]:SC["S3"] + 1],
                                          sc[:, 20:21], 1.0 / QMAX, 1e-12,
                                          OP.mult, OP.max))
                _vs(c, v, v.reciprocal(sc[:, 21:22],
                                       sc[:, SC["S3"]:SC["S3"] + 1]))
                s_v.inc(v.tensor_mul(
                    sc[:, SC["Q3"]:SC["Q3"] + 1], sc[:, 22:23],
                    sc[:, 21:22]))
                _vs(c, v, v.memset(sc[:, 13:14], 0.0))
            else:
                _vs(c, v, v.tensor_mul(sc[:, 23:24],
                                       sc[:, SC["S3"]:SC["S3"] + 1],
                                       sc[:, SC["S2U"]:SC["S2U"] + 1]))
                _vs(c, v, v.tensor_mul(sc[:, 20:21], g0, sc[:, 23:24]))
                _vs(c, v, v.tensor_scalar(sc[:, SC["S4"]:SC["S4"] + 1],
                                          sc[:, 20:21], 1.0 / QMAX, 1e-12,
                                          OP.mult, OP.max))
                _vs(c, v, v.reciprocal(sc[:, 21:22],
                                       sc[:, SC["S4"]:SC["S4"] + 1]))
                _vs(c, v, v.tensor_mul(sc[:, SC["Q4"]:SC["Q4"] + 1],
                                       sc[:, 23:24], sc[:, 21:22]))
                s_v.inc(v.tensor_scalar(
                    P["gbs4"][:, :], P["gb"][:, :],
                    sc[:, SC["S4"]:SC["S4"] + 1], None, OP.mult))


def _xqpass(c):
    nc, P, AB, AF4 = c.nc, c.P, c.AB, c.AF
    s_ld, s_v = c.sems["s_ld"], c.sems["s_v"]
    SC = c.SC
    h2_sp = c.io["h2_sp"].ap()
    xq = AB.view(c.o_xq, 128, NCH, BL * T)
    h2t = AB.view(c.o_ffn2, 128, 2, TSL)
    ft = AF4.view(c.f_ffn, 128, TSL)
    with nc.Block() as blk:
        ld0, v0 = s_ld.v, s_v.v
        iters = [(s, ch, sl) for s in range(BL) for ch in range(NCH)
                 for sl in range(NSL)]

        @blk.sync
        def _(sy):
            for it, (s, ch, sl) in enumerate(iters):
                if it >= 2:
                    sy.wait_ge(s_v.h, v0 + it - 1)
                s_ld.dma(sy.dma_start(
                    out=h2t[:, it % 2, :],
                    in_=h2_sp[s, ch, :, sl * TSL:(sl + 1) * TSL]))

        @blk.vector
        def _(v):
            for it, (s, ch, sl) in enumerate(iters):
                v.wait_ge(s_ld.h, ld0 + (it + 1) * 16)
                v.tensor_scalar(ft, h2t[:, it % 2, :],
                                P["sc"][:, SC["SXI"]:SC["SXI"] + 1], None,
                                OP.mult)
                v.tensor_scalar(ft, ft, MAGIC, None, OP.add)
                s_v.inc(v.tensor_scalar(
                    xq[:, ch, s * T + sl * TSL:s * T + (sl + 1) * TSL], ft,
                    -MAGIC, None, OP.add))


def _f1(c):
    nc, P, AB, AF4, PS = c.nc, c.P, c.AB, c.AF, c.PS
    s_tl, s_pe, s_v, s_gd = (c.sems[k] for k in
                             ("s_tl", "s_pe", "s_v", "s_gd"))
    xq = AB.view(c.o_xq, 128, NCH, BL * T)
    wst = AB.view(c.o_wst, 128, NCH, FI)
    fst = AB.view(c.o_fst, 128, 8, TSL)
    tmp1 = AF4.view(c.f_sm, 128, 1)
    NTG = BL * T // TSL
    for wnm, spnm, mxcol in (("wu_dq", "up_sp", 1), ("wg_dq", "gate_sp", 2)):
        sp = c.io[spnm].ap()
        with nc.Block() as blk:
            @blk.scalar
            def _(se):
                for ci in range(NCH):
                    s_tl.dma(se.dma_start_transpose(
                        wst[:, ci, :],
                        c.io[wnm].ap()[:, ci * 128:(ci + 1) * 128]))
                s_tl.wait(se)

        with nc.Block() as blk:
            pe0, v0, gd0 = s_pe.v, s_v.v, s_gd.v

            @blk.tensor
            def _(t):
                for oc in range(FI // 128):
                    if oc >= 1:
                        t.wait_ge(s_v.h, v0 + oc * 2 * NTG)
                    for ci in range(NCH):
                        for tg in range(NTG):
                            ins = t.matmul(
                                PS[tg][:, 0:TSL],
                                wst[:, ci, oc * 128:(oc + 1) * 128],
                                xq[:, ci, tg * TSL:(tg + 1) * TSL],
                                start=(ci == 0), stop=(ci == NCH - 1))
                    s_pe.inc(ins)

            @blk.vector
            def _(v):
                for oc in range(FI // 128):
                    v.wait_ge(s_pe.h, pe0 + oc + 1)
                    if oc >= 2:
                        v.wait_ge(s_gd.h, gd0 + (oc - 1) * 16)
                    for tg in range(NTG):
                        v.tensor_copy(fst[:, tg, :], PS[tg][:, 0:TSL])
                        mc = oc * NTG + tg
                        ins = v.tensor_reduce(P["mxt"][:, mc:mc + 1],
                                              PS[tg][:, 0:TSL], AX.X, OP.max,
                                              apply_absolute_value=True)
                    s_v.inc(ins, 2 * NTG)
                    if oc == FI // 128 - 1:
                        _vs(c, v, v.memset(P["sc"][:, 13:14], 0.0))
                        _vs(c, v, v.tensor_reduce(
                            P["mxc"][:, mxcol:mxcol + 1],
                            P["mxt"][:, 0:FI // 128 * NTG], AX.X, OP.max))

            @blk.gpsimd
            def _(gp):
                for oc in range(FI // 128):
                    gp.wait_ge(s_v.h, v0 + (oc + 1) * 2 * NTG)
                    s_gd.dma(gp.dma_start(
                        out=sp[oc, :, :],
                        in_=fst[:, :, :].rearrange("p a b -> p (a b)")))
                s_gd.wait(gp)


def _mid1(c):
    nc, P, AB, AF4 = c.nc, c.P, c.AB, c.AF
    s_ld, s_v, s_a, s_gd = (c.sems[k] for k in
                            ("s_ld", "s_v", "s_a", "s_gd"))
    SC = c.SC
    gsp = c.io["gate_sp"].ap()
    g1sp = c.io["g1s_sp"].ap()
    gt = AB.view(c.o_ffn2, 128, 2, TSL)
    g1t = AB.view(c.o_ffn2 + 2 * TSL, 128, 2, TSL)
    qg = AF4.view(c.f_ffn, 128, TSL)
    sig = AF4.view(c.f_ffn + TSL, 128, TSL)
    tmp1 = AF4.view(c.f_sm, 128, 1)
    NTG = BL * T // TSL
    with nc.Block() as blk:
        ld0, v0, a0, gd0 = s_ld.v, s_v.v, s_a.v, s_gd.v
        iters = [(oc, tg) for oc in range(FI // 128) for tg in range(NTG)]

        @blk.sync
        def _(sy):
            for it, (oc, tg) in enumerate(iters):
                if it >= 2:
                    sy.wait_ge(s_v.h, v0 + (it - 1) * 2)
                s_ld.dma(sy.dma_start(
                    out=gt[:, it % 2, :],
                    in_=gsp[oc, :, tg * TSL:(tg + 1) * TSL]))

        @blk.vector
        def _(v):
            for it, (oc, tg) in enumerate(iters):
                v.wait_ge(s_ld.h, ld0 + (it + 1) * 16)
                v.tensor_scalar(qg, gt[:, it % 2, :],
                                P["sc"][:, SC["QG"]:SC["QG"] + 1], None,
                                OP.mult)
                v.tensor_scalar(qg, qg, MAGIC, None, OP.add)
                s_v.inc(v.tensor_scalar(qg, qg, -MAGIC, None, OP.add))
                v.wait_ge(s_a.h, a0 + it + 1)
                v.tensor_scalar(sig, sig, 65536.0, None, OP.mult)
                v.tensor_scalar(sig, sig, MAGIC, None, OP.add)
                v.tensor_scalar(sig, sig, -MAGIC, 65535.0, OP.add, OP.min)
                if it >= 2:
                    v.wait_ge(s_gd.h, gd0 + (it - 1) * 16)
                s_v.inc(v.tensor_mul(g1t[:, it % 2, :], qg, sig))
                ins = v.tensor_reduce(P["mxt"][:, it:it + 1],
                                      g1t[:, it % 2, :], AX.X, OP.max,
                                      apply_absolute_value=True)
                if it == len(iters) - 1:
                    _vs(c, v, ins)
                    _vs(c, v, v.tensor_reduce(
                        P["mxc"][:, 3:4], P["mxt"][:, 0:len(iters)],
                        AX.X, OP.max))

        @blk.scalar
        def _(se):
            for it, (oc, tg) in enumerate(iters):
                se.wait_ge(s_v.h, v0 + it * 2 + 1)
                s_a.inc(se.activation(sig, qg, AF.Sigmoid,
                                      scale=P["sc"][:, SC["S2G"]:
                                                    SC["S2G"] + 1]))

        @blk.gpsimd
        def _(gp):
            for it, (oc, tg) in enumerate(iters):
                gp.wait_ge(s_v.h, v0 + (it + 1) * 2)
                s_gd.dma(gp.dma_start(
                    out=g1sp[oc, :, tg * TSL:(tg + 1) * TSL],
                    in_=g1t[:, it % 2, :]))
            s_gd.wait(gp)


def _mid2(c):
    nc, P, AB, AF4 = c.nc, c.P, c.AB, c.AF
    s_ld, s_v, s_gd = (c.sems[k] for k in ("s_ld", "s_v", "s_gd"))
    SC = c.SC
    g1sp = c.io["g1s_sp"].ap()
    usp = c.io["up_sp"].ap()
    g2sp = c.io["g2u_sp"].ap()
    g1t = AB.view(c.o_ffn2, 128, 2, TSL)
    ut = AB.view(c.o_ffn2 + 2 * TSL, 128, 2, TSL)
    g2t = AB.view(c.o_ffn2 + 4 * TSL, 128, 2, TSL)
    q3 = AF4.view(c.f_ffn, 128, TSL)
    qu = AF4.view(c.f_ffn + TSL, 128, TSL)
    tmp1 = AF4.view(c.f_sm, 128, 1)
    NTG = BL * T // TSL
    with nc.Block() as blk:
        ld0, v0, gd0 = s_ld.v, s_v.v, s_gd.v
        iters = [(oc, tg) for oc in range(FI // 128) for tg in range(NTG)]

        @blk.sync
        def _(sy):
            for it, (oc, tg) in enumerate(iters):
                if it >= 2:
                    sy.wait_ge(s_v.h, v0 + (it - 1))
                s_ld.dma(sy.dma_start(
                    out=g1t[:, it % 2, :],
                    in_=g1sp[oc, :, tg * TSL:(tg + 1) * TSL]))
                s_ld.dma(sy.dma_start(
                    out=ut[:, it % 2, :],
                    in_=usp[oc, :, tg * TSL:(tg + 1) * TSL]))

        @blk.vector
        def _(v):
            for it, (oc, tg) in enumerate(iters):
                v.wait_ge(s_ld.h, ld0 + (it + 1) * 32)
                v.tensor_scalar(q3, g1t[:, it % 2, :],
                                P["sc"][:, SC["Q3"]:SC["Q3"] + 1], None,
                                OP.mult)
                v.tensor_scalar(q3, q3, MAGIC, None, OP.add)
                v.tensor_scalar(q3, q3, -MAGIC, None, OP.add)
                v.tensor_scalar(qu, ut[:, it % 2, :],
                                P["sc"][:, SC["QU"]:SC["QU"] + 1], None,
                                OP.mult)
                v.tensor_scalar(qu, qu, MAGIC, None, OP.add)
                v.tensor_scalar(qu, qu, -MAGIC, None, OP.add)
                if it >= 2:
                    v.wait_ge(s_gd.h, gd0 + (it - 1) * 16)
                v.tensor_mul(g2t[:, it % 2, :], q3, qu)
                ins = v.tensor_reduce(P["mxt"][:, it:it + 1],
                                      g2t[:, it % 2, :], AX.X, OP.max,
                                      apply_absolute_value=True)
                s_v.inc(ins)
                if it == len(iters) - 1:
                    _vs(c, v, v.memset(P["sc"][:, 13:14], 0.0))
                    _vs(c, v, v.tensor_reduce(
                        P["mxc"][:, 4:5], P["mxt"][:, 0:len(iters)],
                        AX.X, OP.max))

        @blk.gpsimd
        def _(gp):
            for it, (oc, tg) in enumerate(iters):
                gp.wait_ge(s_v.h, v0 + it + 1)
                s_gd.dma(gp.dma_start(
                    out=g2sp[oc, :, tg * TSL:(tg + 1) * TSL],
                    in_=g2t[:, it % 2, :]))
            s_gd.wait(gp)


def _down(c):
    nc, P, AB, AF4, PS = c.nc, c.P, c.AB, c.AF, c.PS
    s_tl, s_ld, s_pe, s_v, s_gd, s_out = (
        c.sems[k] for k in ("s_tl", "s_ld", "s_pe", "s_v", "s_gd", "s_out"))
    SC = c.SC
    g2sp = c.io["g2u_sp"].ap()
    x2_sp = c.io["x2_sp"].ap()
    outT = c.io["outT"].ap()
    wd = AB.view(c.o_wd, 128, FI // 128, D)
    g2u = AB.view(c.o_g2u, 128, FI // 128, TSL)
    oq = AB.view(c.o_oq, 128, FI // 128, TSL)
    ft = AF4.view(c.f_ffn, 128, TSL)
    x2t = AF4.view(c.f_x2t, 128, 2, TSL)
    x3t = AF4.view(c.f_xstg, 128, 2, TSL)
    NTG = BL * T // TSL
    with nc.Block() as blk:
        @blk.scalar
        def _(se):
            for hc in range(FI // 128):
                s_tl.dma(se.dma_start_transpose(
                    wd[:, hc, :],
                    c.io["wd_dq"].ap()[:, hc * 128:(hc + 1) * 128]))
            s_tl.wait(se)

    for ts in range(NTG):
        s = ts // NSL
        with nc.Block() as blk:
            pe0, v0, ld0 = s_pe.v, s_v.v, s_ld.v

            @blk.sync
            def _(sy):
                s_ld.dma(sy.dma_start(
                    out=g2u[:, :, :],
                    in_=g2sp[:, :, ts * TSL:(ts + 1) * TSL]
                    .rearrange("cc p t -> p cc t")))
                for mo in range(NCH):
                    if mo >= 2:
                        sy.wait_ge(s_v.h, v0 + 1 + mo - 1)
                    s_ld.dma(sy.dma_start(
                        out=x2t[:, mo % 2, :],
                        in_=x2_sp[s, mo, :,
                                  (ts % NSL) * TSL:(ts % NSL + 1) * TSL]))

            @blk.vector
            def _(v):
                out0 = s_out.v
                v.wait_ge(s_ld.h, ld0 + 16)
                for hc in range(FI // 128):
                    v.tensor_scalar(ft, g2u[:, hc, :],
                                    P["sc"][:, SC["Q4"]:SC["Q4"] + 1], None,
                                    OP.mult)
                    v.tensor_scalar(ft, ft, MAGIC, None, OP.add)
                    ins = v.tensor_scalar(oq[:, hc, :], ft, -MAGIC, None,
                                          OP.add)
                s_v.inc(ins)
                for mo in range(NCH):
                    if mo >= 2:
                        v.wait_ge(s_out.h, out0 + (mo - 1) * 16)
                    v.wait_ge(s_pe.h, pe0 + mo + 1)
                    v.wait_ge(s_ld.h, ld0 + 16 + (mo + 1) * 16)
                    cb = mo * BL + s
                    v.tensor_scalar(ft, PS[mo % 4][:, 0:TSL],
                                    P["gbs4"][:, cb:cb + 1], None, OP.mult)
                    s_v.inc(v.tensor_add(x3t[:, mo % 2, :], ft,
                                         x2t[:, mo % 2, :]))

            @blk.tensor
            def _(t):
                t.wait_ge(s_v.h, v0 + 1)
                for mo in range(NCH):
                    if mo >= 4:
                        t.wait_ge(s_v.h, v0 + 1 + (mo - 3))
                    for hc in range(FI // 128):
                        ins = t.matmul(PS[mo % 4][:, 0:TSL],
                                       wd[:, hc, mo * 128:(mo + 1) * 128],
                                       oq[:, hc, :],
                                       start=(hc == 0),
                                       stop=(hc == FI // 128 - 1))
                    s_pe.inc(ins)

            @blk.gpsimd
            def _(gp):
                for mo in range(NCH):
                    gp.wait_ge(s_v.h, v0 + 1 + mo + 1)
                    s_out.dma(gp.dma_start(
                        out=outT[s, mo * 128:(mo + 1) * 128,
                                 (ts % NSL) * TSL:(ts % NSL + 1) * TSL],
                        in_=x3t[:, mo % 2, :]))
                s_out.wait(gp)


def stage_inputs(inputs):
    x = np.ascontiguousarray(np.asarray(inputs["L0_attn_x"], np.float32))
    w_q = np.ascontiguousarray(np.asarray(inputs["L0_attn_w_q"], np.float32))
    w_kv = np.ascontiguousarray(np.asarray(inputs["L0_attn_w_kv"], np.float32))
    w_out = np.ascontiguousarray(np.asarray(inputs["L0_attn_w_out"], np.float32))
    b_out = np.asarray(inputs["L0_attn_b_out"], np.float32)
    sort_w = np.asarray(inputs["L0_attn_sort_w"], np.float32)
    w_gate = np.ascontiguousarray(np.asarray(inputs["L0_ffn_w_gate"], np.float32))
    w_up = np.ascontiguousarray(np.asarray(inputs["L0_ffn_w_up"], np.float32))
    w_down = np.ascontiguousarray(np.asarray(inputs["L0_ffn_w_down"], np.float32))
    cc = np.asarray(inputs["L0_c"], np.float32)
    wmod = np.ascontiguousarray(np.asarray(inputs["L0_wmod"], np.float32))

    b_out_t = np.ascontiguousarray(b_out.reshape(NCH, 128).T)
    sw = sort_w.reshape(SHD, 2, 64, 64)
    sw2 = np.ascontiguousarray(
        np.broadcast_to(sw[None], (2, SHD, 2, 64, 64))
        .transpose(0, 3, 1, 2, 4))
    cT = np.ascontiguousarray(cc.T)

    in_maps = []
    for core in range(NCORES):
        b0 = core * BL
        in_maps.append({
            "xT": np.ascontiguousarray(x[b0:b0 + BL].transpose(0, 2, 1)),
            "w_q": w_q, "w_kv": w_kv, "w_out": w_out, "b_out": b_out_t,
            "sort_w": sw2, "cT": np.ascontiguousarray(cT[:, b0:b0 + BL]),
            "wmod": wmod,
            "wg_nat": w_gate, "wu_nat": w_up, "wd_nat": w_down,
        })
    return in_maps


def kernel(**inputs):
    in_maps = stage_inputs(inputs)
    nc, io, sems = build_nc()
    res = run_bass_kernel_spmd(nc, in_maps, core_ids=list(range(NCORES)))
    outs = [np.asarray(r["outT"]).transpose(0, 2, 1) for r in res.results]
    return np.ascontiguousarray(np.concatenate(outs, axis=0)).astype(np.float32)
